# version 14
# baseline (speedup 1.0000x reference)
import numpy as np
from contextlib import ExitStack

import concourse.bass as bass
import concourse.mybir as mybir
from concourse.bass_utils import run_bass_kernel_spmd

F32 = mybir.dt.float32
BF16 = mybir.dt.bfloat16
AF = mybir.ActivationFunctionType
OP = mybir.AluOpType
AX = mybir.AxisListType

D = 1024
S = 4096
E = 8
NDT = 8
NST = 2
STW = 2048
NTT = 16
NT4 = 4
NSLAB = 8
NFBL = 32
URING = 9
EPS = 1e-5
SCALE = 0.5 ** 0.5
NK = NST * E * NSLAB

ENGINES = ("sync", "tensor", "vector", "scalar", "gpsimd")


class Sched:

    def __init__(self):
        self.ops = {e: [] for e in ENGINES}
        self.cnt = {}

    def now(self, sem):
        return self.cnt.get(sem, 0)

    def op(self, eng, fn, waits=(), incs=()):
        waits = [(s, v) for (s, v) in waits if v > 0]
        rec = []
        for s, k in incs:
            self.cnt[s] = self.cnt.get(s, 0) + k
            rec.append((s, k, self.cnt[s]))
        self.ops[eng].append((fn, waits, [(s, k) for (s, k, _) in rec]))
        return {s: v for (s, k, v) in rec}


def build_program(debug=False):
    nc = bass.Bass("TRN2", target_bir_lowering=False)

    x_h = nc.dram_tensor("x", [S, D], F32, kind="ExternalInput")
    w1a_h = nc.dram_tensor("w1a", [E, NSLAB, 128, 4096], F32, kind="ExternalInput")
    w1g_h = nc.dram_tensor("w1g", [E, NSLAB, 128, 4096], F32, kind="ExternalInput")
    w2_h = nc.dram_tensor("w2", [E, NSLAB, 128, 4096], F32, kind="ExternalInput")
    rw_h = nc.dram_tensor("rw", [128, 64], F32, kind="ExternalInput")
    rbb_h = nc.dram_tensor("rbb", [128, 8], F32, kind="ExternalInput")
    prm_h = nc.dram_tensor("prm", [128, 72], F32, kind="ExternalInput")
    idn_h = nc.dram_tensor("idn", [128, 128], F32, kind="ExternalInput")
    out_h = nc.dram_tensor("out", [S, D], F32, kind="ExternalOutput")

    es = ExitStack()

    def SB(name, sh, dt=F32):
        return es.enter_context(nc.sbuf_tensor(name, sh, dt))

    RES = SB("RES", [128, NTT * 1024])
    WA = SB("WA", [128, 2 * 4096], BF16)
    WG = SB("WG", [128, 2 * 4096], BF16)
    W2S = SB("W2S", [128, 3 * 4096], BF16)
    SILB = SB("SILB", [128, 1024])
    LGTS = SB("LGTS", [128, NTT * 8])
    STAT = SB("STAT", [128, 48])
    CSB = SB("CSB", [128, 8])
    CSR = SB("CSR", [1, 8])
    ONESR = SB("ONESR", [1, 128])
    MS = SB("MS", [128, NTT * 8])
    QC = SB("QC", [128, 36])
    PRM = SB("PRM", [128, 72])
    RW = SB("RW", [128, 64])
    RBB = SB("RBB", [128, 8])
    IDN = SB("IDN", [128, 128])
    CAR = SB("CAR", [128, 16])
    ARENA = SB("ARENA", [128, 81920], mybir.dt.uint8)

    abase = nc.lookup_mloc(ARENA).addr
    XD = nc.alloc_sbuf_tensor_at("XD", [128, NDT * STW], F32, offset=abase)
    H1 = nc.alloc_sbuf_tensor_at("H1", [128, STW], F32, offset=abase + 65536)
    H2 = nc.alloc_sbuf_tensor_at("H2", [128, STW], F32, offset=abase + 65536 + 8192)
    XT = nc.alloc_sbuf_tensor_at("XT", [128, NDT * STW], BF16, offset=abase)
    UU = nc.alloc_sbuf_tensor_at("UU", [128, URING * STW], BF16, offset=abase + 32768)
    SIL = (SILB[:, 0:512], SILB[:, 512:1024])

    AG = [es.enter_context(nc.psum_tensor(f"AG{i}", [128, 512], F32)) for i in range(4)]
    Y = [es.enter_context(nc.psum_tensor(f"Y{i}", [128, 512], F32)) for i in range(2)]
    TR = [es.enter_context(nc.psum_tensor(f"TR{i}", [128, 512], F32)) for i in range(2)]

    sems = {}
    for sname in (["ld", "pe", "ve", "ac", "u", "st",
                   "wa0", "wa1", "wg0", "wg1", "w2_0", "w2_1", "w2_2"]
                  + [f"x{t}" for t in range(NTT)]):
        sems[sname] = es.enter_context(nc.semaphore(sname))

    sch = Sched()

    tr_free = [0, 0]
    a_free = [0, 0]
    g_free = [0, 0]
    sil_free = [0, 0]
    y_free = [0, 0]
    silu_h1_ac = [0]
    slabA_end_pe = {}
    slabB_end_pe = {}

    def dbg_dump(name, buf, free_n, waits):
        if not debug:
            return 0
        h = nc.dram_tensor("dbg_" + name, [128, free_n], F32, kind="ExternalOutput")
        sch.op("sync", (lambda sy, h=h, buf=buf, free_n=free_n:
                        sy.dma_start(h[:, :], buf[:, 0:free_n])),
               waits=waits, incs=[("st", 16)])
        return sch.now("st")

    def emit_slab_load(k):
        e_i = (k % (E * NSLAB)) // NSLAB
        s_i = k % NSLAB
        w_a = [("pe", slabA_end_pe[k - 2])] if k >= 2 else []
        w_b = [("pe", slabB_end_pe[k - 3])] if k >= 3 else []
        pa = (k % 2) * 4096
        pb = (k % 3) * 4096

        def f_wa(g, e_i=e_i, s_i=s_i, pa=pa):
            return g.dma_start(WA[:, pa:pa + 4096], w1a_h[e_i, s_i, :, :])

        def f_wg(g, e_i=e_i, s_i=s_i, pa=pa):
            return g.dma_start(WG[:, pa:pa + 4096], w1g_h[e_i, s_i, :, :])

        def f_w2(g, e_i=e_i, s_i=s_i, pb=pb):
            return g.dma_start(W2S[:, pb:pb + 4096], w2_h[e_i, s_i, :, :])

        sch.op("gpsimd", f_wa, waits=w_a, incs=[(f"wa{k % 2}", 16)])
        sch.op("gpsimd", f_wg, incs=[(f"wg{k % 2}", 16)])
        sch.op("gpsimd", f_w2, waits=w_b, incs=[(f"w2_{k % 3}", 16)])

    def f_ld(sy, src, dst):
        return sy.dma_start(dst, src)

    for src, dst in ((prm_h[:, :], PRM[:, :]), (rw_h[:, :], RW[:, :]),
                     (rbb_h[:, :], RBB[:, :]), (idn_h[:, :], IDN[:, :])):
        sch.op("sync", (lambda sy, s=src, d=dst: sy.dma_start(d, s)), incs=[("ld", 16)])

    emit_slab_load(0)
    emit_slab_load(1)

    def vchain(fns, first_waits=()):
        waits = list(first_waits)
        for fn in fns:
            sch.op("vector", fn, waits=waits, incs=[("ve", 1)])
            waits = [("ve", sch.now("ve"))]
        return sch.now("ve")

    sch.op("scalar", lambda s: s.activation(SILB[:, 0:16], PRM[:, 0:16], AF.Sigmoid),
           waits=[("ld", 64)])
    sch.op("scalar", lambda s: s.activation(SILB[:, 16:32], PRM[:, 16:32], AF.Sigmoid),
           incs=[("ac", 1)])
    ac_sig = sch.now("ac")
    ve_qc = vchain([
        lambda v: v.tensor_mul(QC[:, 0:16], SILB[:, 0:16], SILB[:, 16:32]),
        lambda v: v.tensor_scalar_mul(QC[:, 0:16], QC[:, 0:16], -1.0),
        lambda v: v.tensor_scalar_add(QC[:, 0:16], QC[:, 0:16], 1.0),
        lambda v: v.tensor_mul(QC[:, 16:32], SILB[:, 0:16], PRM[:, 32:48]),
        lambda v: v.tensor_mul(QC[:, 16:32], QC[:, 16:32], PRM[:, 48:64]),
        lambda v: v.tensor_scalar_mul(QC[:, 16:32], QC[:, 16:32], SCALE),
        lambda v: v.memset(QC[:, 32:33], 1.0),
        lambda v: v.memset(QC[:, 33:34], EPS),
        lambda v: v.memset(QC[:, 34:35], 1.0),
        lambda v: v.memset(ONESR[:, :], 1.0),
    ], first_waits=[("ac", ac_sig)])
    def f_cs(t):
        last = None
        for dt in range(NDT):
            last = t.matmul(TR[0][0:1, 0:8], QC[:, 34:35], RW[:, dt * 8:dt * 8 + 8],
                            start=(dt == 0), stop=(dt == NDT - 1))
        return last
    sch.op("tensor", f_cs, waits=[("ld", 64), ("ve", ve_qc)], incs=[("pe", 1)])
    sch.op("scalar", lambda s: s.copy(CSR[:, :], TR[0][0:1, 0:8]),
           waits=[("pe", sch.now("pe"))], incs=[("ac", 1)])
    sch.op("tensor", (lambda t: t.matmul(TR[1][0:128, 0:8], ONESR[:, :], CSR[:, :],
                                         start=True, stop=True)),
           waits=[("ac", sch.now("ac"))], incs=[("pe", 1)])
    sch.op("scalar", lambda s: s.copy(CSB[:, :], TR[1][:, 0:8]),
           waits=[("pe", sch.now("pe"))], incs=[("ac", 1)])
    tr_free[0] = sch.now("ac")
    tr_free[1] = sch.now("ac")

    ffn_end_prev = 0

    for st in range(NST):
        if st == 0:
            for tt in range(NTT):
                r0 = tt * 128
                sch.op("sync",
                       (lambda sy, r0=r0, tt=tt:
                        sy.dma_start(RES[:, tt * 1024:(tt + 1) * 1024], x_h[r0:r0 + 128, :])),
                       incs=[(f"x{tt}", 16)])

        t1_dt_done = []
        for dt in range(NDT):
            for tt in range(NTT):
                p = (dt * NTT + tt) % 2
                waits = [("ac", tr_free[p])]
                if dt == 0:
                    waits.append((f"x{tt}", (st + 1) * 16))
                if st == 0 and tt == 0 and dt == 0:
                    waits.append(("ld", 64))
                sch.op("tensor",
                       (lambda t, tt=tt, dt=dt, p=p:
                        t.transpose(TR[p][:, 0:128],
                                    RES[:, tt * 1024 + dt * 128: tt * 1024 + dt * 128 + 128],
                                    IDN[:, :])),
                       waits=waits, incs=[("pe", 1)])
                pe_c = sch.now("pe")
                sch.op("scalar",
                       (lambda s, tt=tt, dt=dt, p=p:
                        s.activation(XD[:, dt * STW + tt * 128: dt * STW + tt * 128 + 128],
                                     TR[p][:, 0:128], AF.Copy)),
                       waits=[("pe", pe_c)], incs=[("ac", 1)])
                tr_free[p] = sch.now("ac")
            t1_dt_done.append(sch.now("ac"))

        gate_scan = (dbg_dump(f"xt1_{st}", XD, NDT * STW, [("ac", t1_dt_done[-1])])
                     if st == 0 and debug else 0)
        sch.op("vector", lambda v: v.engine_nop(),
               waits=[("ve", ve_qc)] + ([("st", gate_scan)] if gate_scan else []))
        t2_tt_done_tmp = []
        for dt in range(NDT):
            xd = lambda dt=dt: XD[:, dt * STW:(dt + 1) * STW]
            q1 = QC[:, 2 * dt:2 * dt + 1].broadcast_to([128, STW])
            q2 = QC[:, 2 * dt + 1:2 * dt + 2].broadcast_to([128, STW])
            init1 = 0.0 if st == 0 else CAR[:, 2 * dt:2 * dt + 1]
            init2 = 0.0 if st == 0 else CAR[:, 2 * dt + 1:2 * dt + 2]
            sch.op("vector", (lambda v, dt=dt, q1=q1, init1=init1:
                              v.tensor_tensor_scan(H1[:, :], q1, XD[:, dt * STW:(dt + 1) * STW],
                                                   init1, OP.mult, OP.add)),
                   waits=[("ac", t1_dt_done[dt]), ("ac", silu_h1_ac[0])],
                   incs=[("ve", 1)])
            e1 = sch.now("ve")
            sch.op("vector", (lambda v, dt=dt, q2=q2, init2=init2:
                              v.tensor_tensor_scan(H2[:, :], q2, XD[:, dt * STW:(dt + 1) * STW],
                                                   init2, OP.mult, OP.add)),
                   waits=[("ve", e1)], incs=[("ve", 1)])
            e2 = sch.now("ve")
            if st == 0:
                vchain([
                    (lambda v, dt=dt:
                     v.tensor_copy(CAR[:, 2 * dt:2 * dt + 1], H1[:, STW - 1:STW])),
                    (lambda v, dt=dt:
                     v.tensor_copy(CAR[:, 2 * dt + 1:2 * dt + 2], H2[:, STW - 1:STW])),
                ], first_waits=[("ve", e2)])
                e2 = sch.now("ve")
            vchain([
                (lambda v, dt=dt:
                 v.tensor_scalar_mul(H1[:, :], H1[:, :], QC[:, 16 + 2 * dt:17 + 2 * dt])),
                (lambda v, dt=dt:
                 v.scalar_tensor_tensor(H1[:, :], H2[:, :],
                                        QC[:, 17 + 2 * dt:18 + 2 * dt], H1[:, :],
                                        OP.mult, OP.add)),
                (lambda v, dt=dt:
                 v.scalar_tensor_tensor(H1[:, :], XD[:, dt * STW:(dt + 1) * STW],
                                        PRM[:, 64 + dt:65 + dt], H1[:, :],
                                        OP.mult, OP.add)),
            ], first_waits=[("ve", e2)])
            e3 = sch.now("ve")
            sch.op("scalar", lambda s: s.activation(H2[:, :], H1[:, :], AF.Silu),
                   waits=[("ve", e3)], incs=[("ac", 1)])
            ac1 = sch.now("ac")
            silu_h1_ac[0] = ac1
            sch.op("vector", (lambda v, dt=dt:
                              v.tensor_add(XD[:, dt * STW:(dt + 1) * STW],
                                           XD[:, dt * STW:(dt + 1) * STW], H2[:, :])),
                   waits=[("ac", ac1)], incs=[("ve", 1)])
            e_dt = sch.now("ve")

            for tt in range(NTT):
                p = (dt * NTT + tt) % 2
                waits = [("ac", tr_free[p])]
                if tt == 0:
                    waits.append(("ve", e_dt))
                sch.op("tensor",
                       (lambda t, tt=tt, dt=dt, p=p:
                        t.transpose(TR[p][:, 0:128],
                                    XD[:, dt * STW + tt * 128: dt * STW + tt * 128 + 128],
                                    IDN[:, :])),
                       waits=waits, incs=[("pe", 1)])
                pe_c = sch.now("pe")
                sch.op("scalar",
                       (lambda s, tt=tt, dt=dt, p=p:
                        s.activation(RES[:, tt * 1024 + dt * 128: tt * 1024 + dt * 128 + 128],
                                     TR[p][:, 0:128], AF.Copy)),
                       waits=[("pe", pe_c)], incs=[("ac", 1)])
                tr_free[p] = sch.now("ac")
                if dt == NDT - 1:
                    t2_tt_done_tmp.append(sch.now("ac"))

        scan_end = sch.now("ve")

        for tt in range(NTT):
            yb = tt % 2
            waits = [("ve", y_free[yb])]
            if tt == 0:
                waits.append(("ve", scan_end))

            def f_lg(t, tt=tt, yb=yb):
                last = None
                for dt in range(NDT):
                    last = t.matmul(Y[yb][:, 0:8],
                                    XD[:, dt * STW + tt * 128: dt * STW + tt * 128 + 128],
                                    RW[:, dt * 8:dt * 8 + 8],
                                    start=(dt == 0), stop=(dt == NDT - 1))
                return last

            sch.op("tensor", f_lg, waits=waits, incs=[("pe", 1)])
            sch.op("vector",
                   (lambda v, tt=tt, yb=yb:
                    v.tensor_copy(LGTS[:, tt * 8:(tt + 1) * 8], Y[yb][:, 0:8])),
                   waits=[("pe", sch.now("pe"))], incs=[("ve", 1)])
            y_free[yb] = sch.now("ve")
        t2_tt_done = t2_tt_done_tmp
        gate_router_pe = (dbg_dump(f"preln_{st}", XD, NDT * STW, [("ve", scan_end)])
                          if st == 0 and debug else 0)

        def emit_ln(tt_ready, gate_act=0):
            done = []
            for h in range(2):
                lo = h * 8
                for tt in range(lo, lo + 8):
                    ws = [tt_ready[tt], ("ac", sch.now("ac"))]
                    if tt == 0 and gate_act:
                        ws.append(("st", gate_act))
                    sch.op("scalar",
                           (lambda s, tt=tt:
                            s.activation(SILB[:, 0:1024], RES[:, tt * 1024:(tt + 1) * 1024],
                                         AF.Square, accum_out=STAT[:, tt:tt + 1])),
                           waits=ws, incs=[("ac", 1)])
                    sch.op("vector",
                           (lambda v, tt=tt:
                            v.reduce_sum(STAT[:, 16 + tt:17 + tt],
                                         RES[:, tt * 1024:(tt + 1) * 1024], axis=AX.X)),
                           waits=[tt_ready[tt]],
                           incs=([("ve", 1)] if tt == lo + 7 else []))
                ac_sq = sch.now("ac")
                rs_done = sch.now("ve")
                vchain([
                    (lambda v, lo=lo: v.tensor_scalar_mul(STAT[:, 16 + lo:24 + lo],
                                                          STAT[:, 16 + lo:24 + lo],
                                                          1.0 / 1024.0)),
                    (lambda v, lo=lo: v.tensor_mul(STAT[:, 32 + lo:40 + lo],
                                                   STAT[:, 16 + lo:24 + lo],
                                                   STAT[:, 16 + lo:24 + lo])),
                    (lambda v, lo=lo: v.tensor_scalar_mul(STAT[:, lo:8 + lo],
                                                          STAT[:, lo:8 + lo],
                                                          1.0 / 1024.0)),
                    (lambda v, lo=lo: v.tensor_sub(STAT[:, lo:8 + lo], STAT[:, lo:8 + lo],
                                                   STAT[:, 32 + lo:40 + lo])),
                ], first_waits=[("ac", ac_sq), ("ve", rs_done)])
                vv = sch.now("ve")
                sch.op("scalar",
                       (lambda s, lo=lo: s.activation(STAT[:, 32 + lo:40 + lo],
                                                      STAT[:, lo:8 + lo],
                                                      AF.Sqrt, bias=QC[:, 33:34])),
                       waits=[("ve", vv)], incs=[("ac", 1)])
                aa = sch.now("ac")
                vchain([
                    (lambda v, lo=lo: v.reciprocal(STAT[:, lo:8 + lo],
                                                   STAT[:, 32 + lo:40 + lo])),
                    (lambda v, lo=lo: v.tensor_mul(STAT[:, 32 + lo:40 + lo],
                                                   STAT[:, 16 + lo:24 + lo],
                                                   STAT[:, lo:8 + lo])),
                    (lambda v, lo=lo: v.tensor_scalar_mul(STAT[:, 32 + lo:40 + lo],
                                                          STAT[:, 32 + lo:40 + lo], -1.0)),
                ], first_waits=[("ac", aa)])
                e2_done = sch.now("ve")
                for tt in range(lo, lo + 8):
                    w = [("ve", e2_done)] if tt == lo else []
                    sch.op("vector",
                           (lambda v, tt=tt:
                            v.tensor_scalar(RES[:, tt * 1024:(tt + 1) * 1024],
                                            RES[:, tt * 1024:(tt + 1) * 1024],
                                            STAT[:, tt:tt + 1], STAT[:, 32 + tt:33 + tt],
                                            OP.mult, OP.add)),
                           waits=w, incs=[("ve", 1)])
                    done.append(sch.now("ve"))
            return done

        ln1_done = emit_ln([("ac", t2_tt_done[tt]) for tt in range(NTT)])

        u_chunk_end = {}
        emitted_passA = set()

        def emit_passA_T(e_i, fbl, T, extra_waits=()):
            if (e_i, fbl, T) in emitted_passA:
                return
            emitted_passA.add((e_i, fbl, T))
            sl = fbl // 4
            fb = fbl % 4
            k = st * (E * NSLAB) + e_i * NSLAB + sl
            kwa = (k % 2) * 4096
            slot = ((st * 8 + e_i) * 32 + fbl) % URING
            pp = fbl % 2
            a_ps, g_ps = AG[2 * pp], AG[2 * pp + 1]
            sp = T % 2
            waits = [("ac", a_free[pp])] + list(extra_waits)
            if fb == 0 and T == 0:
                waits.append((f"wa{k % 2}", 16 * (k // 2 + 1)))
                waits.append((f"wg{k % 2}", 16 * (k // 2 + 1)))

            def f_a(t, kwa=kwa, fb=fb, T=T, a_ps=a_ps):
                last = None
                for dt in range(NDT):
                    last = t.matmul(
                        a_ps[:, :],
                        WA[:, kwa + fb * 1024 + dt * 128: kwa + fb * 1024 + dt * 128 + 128],
                        XT[:, dt * STW + T * 512: dt * STW + T * 512 + 512],
                        start=(dt == 0), stop=(dt == NDT - 1))
                return last

            sch.op("tensor", f_a, waits=waits, incs=[("pe", 1)])
            a_stop = sch.now("pe")
            sch.op("scalar",
                   (lambda s, sp=sp, a_ps=a_ps:
                    s.activation(SIL[sp], a_ps[:, :], AF.Silu)),
                   waits=[("pe", a_stop), ("u", sil_free[sp])],
                   incs=[("ac", 1)])
            silu_c = sch.now("ac")
            a_free[pp] = silu_c

            def f_g(t, kwa=kwa, fb=fb, T=T, g_ps=g_ps):
                last = None
                for dt in range(NDT):
                    last = t.matmul(
                        g_ps[:, :],
                        WG[:, kwa + fb * 1024 + dt * 128: kwa + fb * 1024 + dt * 128 + 128],
                        XT[:, dt * STW + T * 512: dt * STW + T * 512 + 512],
                        start=(dt == 0), stop=(dt == NDT - 1))
                return last

            sch.op("tensor", f_g, waits=[("u", g_free[pp])], incs=[("pe", 1)])
            g_stop = sch.now("pe")
            if fb == 3 and T == NT4 - 1:
                slabA_end_pe[k] = g_stop
            sch.op("vector",
                   (lambda v, slot=slot, T=T, sp=sp, g_ps=g_ps:
                    v.tensor_tensor(UU[:, slot * STW + T * 512: slot * STW + T * 512 + 512],
                                    SIL[sp], g_ps[:, :], OP.mult)),
                   waits=[("ac", silu_c), ("pe", g_stop)], incs=[("u", 1)])
            uc = sch.now("u")
            g_free[pp] = uc
            sil_free[sp] = uc
            if fbl % 8 == 7 and T == NT4 - 1:
                u_chunk_end[(e_i, fbl // 8)] = sch.now("u")

        def emit_passA(e_i, fbl):
            for T in range(NT4):
                emit_passA_T(e_i, fbl, T)

        gate_ffn_pe = (dbg_dump(f"ln1_{st}", RES, NTT * 1024, [("ve", ln1_done[NTT - 1])])
                       if st == 0 else 0)
        for tt in range(NTT):
            w0 = [("ve", sch.now("ve")), ("ve", ln1_done[tt])]
            sch.op("vector",
                   (lambda v, tt=tt:
                    v.scalar_tensor_tensor(LGTS[:, tt * 8:(tt + 1) * 8],
                                           LGTS[:, tt * 8:(tt + 1) * 8],
                                           STAT[:, tt:tt + 1], RBB[:, :],
                                           OP.mult, OP.add)),
                   waits=w0, incs=[("ve", 1)])
            sch.op("vector",
                   (lambda v, tt=tt:
                    v.scalar_tensor_tensor(LGTS[:, tt * 8:(tt + 1) * 8],
                                           CSB[:, :], STAT[:, 32 + tt:33 + tt],
                                           LGTS[:, tt * 8:(tt + 1) * 8],
                                           OP.mult, OP.add)),
                   waits=[("ve", sch.now("ve"))], incs=[("ve", 1)])
        lgt_read = sch.now("ve")
        if st == 0:
            dbg_dump("lgts0", LGTS, NTT * 8, [("ve", lgt_read)])

        for tt in range(NTT):
            for dt in range(NDT):
                p = (tt * NDT + dt) % 2
                waits = [("ac", tr_free[p])]
                if dt == 0:
                    waits.append(("ve", ln1_done[tt]))
                if tt == 0 and dt == 0 and gate_router_pe:
                    waits.append(("st", gate_router_pe))
                sch.op("tensor",
                       (lambda t, tt=tt, dt=dt, p=p:
                        t.transpose(TR[p][:, 0:128],
                                    RES[:, tt * 1024 + dt * 128: tt * 1024 + dt * 128 + 128],
                                    IDN[:, :])),
                       waits=waits, incs=[("pe", 1)])
                pe_c = sch.now("pe")
                sch.op("scalar",
                       (lambda s, tt=tt, dt=dt, p=p:
                        s.activation(XT[:, dt * STW + tt * 128: dt * STW + tt * 128 + 128],
                                     TR[p][:, 0:128], AF.Copy)),
                       waits=[("pe", pe_c)], incs=[("ac", 1)])
                tr_free[p] = sch.now("ac")
            if tt % 4 == 3:
                Tearly = tt // 4
                ew = [("ac", sch.now("ac"))]
                if Tearly == 0 and gate_ffn_pe:
                    ew.append(("st", gate_ffn_pe))
                emit_passA_T(0, 0, Tearly, extra_waits=ew)
        xt_done = sch.now("ac")

        LGE = H2[:, 0:128]
        EQm = H2[:, 128:256]
        CUM = H2[:, 256:384]
        NMX = H2[:, 384:400]
        SSUM = H2[:, 400:416]
        MAXP = H2[:, 416:432]

        def emit_softmax(lgt_read=lgt_read):
            for tt in range(NTT):
                sch.op("vector",
                       (lambda v, tt=tt:
                        v.reduce_max(NMX[:, tt:tt + 1], LGTS[:, tt * 8:(tt + 1) * 8],
                                     axis=AX.X, negate=True)),
                       waits=([("ve", lgt_read)] if tt == 0 else []),
                       incs=([("ve", 1)] if tt == NTT - 1 else []))
            s1 = sch.now("ve")
            for tt in range(NTT):
                sch.op("scalar",
                       (lambda s, tt=tt:
                        s.activation(LGE[:, tt * 8:(tt + 1) * 8],
                                     LGTS[:, tt * 8:(tt + 1) * 8], AF.Exp,
                                     bias=NMX[:, tt:tt + 1])),
                       waits=([("ve", s1)] if tt == 0 else []),
                       incs=([("ac", 1)] if tt == NTT - 1 else []))
            s2 = sch.now("ac")
            for tt in range(NTT):
                sch.op("vector",
                       (lambda v, tt=tt:
                        v.reduce_sum(SSUM[:, tt:tt + 1], LGE[:, tt * 8:(tt + 1) * 8],
                                     axis=AX.X)),
                       waits=([("ac", s2)] if tt == 0 else []),
                       incs=([("ve", 1)] if tt == NTT - 1 else []))
            vchain([
                lambda v: v.reciprocal(MAXP[:, 0:16], SSUM[:, 0:16]),
            ], first_waits=[("ve", sch.now("ve"))])
            for tt in range(NTT):
                sch.op("vector",
                       (lambda v, tt=tt:
                        v.tensor_scalar(EQm[:, tt * 8:(tt + 1) * 8],
                                        LGTS[:, tt * 8:(tt + 1) * 8],
                                        NMX[:, tt:tt + 1], 0.0, OP.add, OP.is_equal)),
                       incs=([("ve", 1)] if tt == NTT - 1 else []))
            s5 = sch.now("ve")
            for tt in range(NTT):
                sch.op("vector",
                       (lambda v, tt=tt:
                        v.tensor_tensor_scan(CUM[:, tt * 8:(tt + 1) * 8],
                                             QC[:, 32:33].broadcast_to([128, 8]),
                                             EQm[:, tt * 8:(tt + 1) * 8], 0.0,
                                             OP.mult, OP.add)),
                       waits=([("ve", s5)] if tt == 0 else []),
                       incs=([("ve", 1)] if tt == NTT - 1 else []))
            vchain([
                lambda v: v.tensor_sub(CUM[:, 0:128], CUM[:, 0:128], EQm[:, 0:128]),
                lambda v: v.tensor_single_scalar(CUM[:, 0:128], CUM[:, 0:128], 0.0,
                                                 OP.is_equal),
                lambda v: v.tensor_mul(EQm[:, 0:128], EQm[:, 0:128], CUM[:, 0:128]),
                lambda v: v.tensor_tensor(
                    MS[:, :].rearrange("p (t e) -> p t e", e=8),
                    EQm[:, 0:128].rearrange("p (t e) -> p t e", e=8),
                    MAXP[:, 0:16].unsqueeze(2).broadcast_to([128, 16, 8]),
                    OP.mult),
            ], first_waits=[("ve", sch.now("ve"))])

        ffn_done_m = {}
        for e_i in range(E):

            def emit_passB(c):
                slots = [((st * 8 + e_i) * 32 + 8 * c + kk) % URING for kk in range(8)]
                k0 = st * (E * NSLAB) + e_i * NSLAB
                for m in range(NTT):
                    for dh in range(2):
                        yp = (m * 2 + dh) % 2
                        waits = [("ve", y_free[yp])]
                        if m == 0 and dh == 0:
                            waits.append(("u", u_chunk_end[(e_i, c)]))
                            for sl in (2 * c, 2 * c + 1):
                                kk2 = k0 + sl
                                waits.append((f"w2_{kk2 % 3}", 16 * (kk2 // 3 + 1)))

                        def f_y(t, c=c, m=m, dh=dh, yp=yp, slots=slots, k0=k0):
                            last = None
                            for kk in range(8):
                                fbl = 8 * c + kk
                                sl = fbl // 4
                                fbi = fbl % 4
                                kb = ((k0 + sl) % 3) * 4096
                                last = t.matmul(
                                    Y[yp][:, :],
                                    UU[:, slots[kk] * STW + m * 128: slots[kk] * STW + m * 128 + 128],
                                    W2S[:, kb + fbi * 1024 + dh * 512: kb + fbi * 1024 + dh * 512 + 512],
                                    start=(kk == 0), stop=(kk == 7))
                            return last

                        sch.op("tensor", f_y, waits=waits, incs=[("pe", 1)])
                        y_stop = sch.now("pe")
                        stt_waits = [("pe", y_stop)]
                        if e_i == 0 and c == 0 and dh == 0:
                            stt_waits.append(("ve", ln1_done[m]))
                        sch.op("vector",
                               (lambda v, m=m, dh=dh, yp=yp, e_i=e_i:
                                v.scalar_tensor_tensor(
                                    RES[:, m * 1024 + dh * 512: m * 1024 + dh * 512 + 512],
                                    Y[yp][:, :], MS[:, m * 8 + e_i: m * 8 + e_i + 1],
                                    RES[:, m * 1024 + dh * 512: m * 1024 + dh * 512 + 512],
                                    OP.mult, OP.add)),
                               waits=stt_waits, incs=[("ve", 1)])
                        y_free[yp] = sch.now("ve")
                        if e_i == E - 1 and c == 3 and dh == 1:
                            ffn_done_m[m] = sch.now("ve")
                slabB_end_pe[k0 + 2 * c] = sch.now("pe")
                slabB_end_pe[k0 + 2 * c + 1] = sch.now("pe")

            for c in range(4):
                for fbl in range(8 * c, 8 * c + 8):
                    emit_passA(e_i, fbl)
                    if e_i == 0 and fbl == 1:
                        emit_softmax()
                        if st == 0:
                            dbg_dump("ms0", MS, NTT * 8, [("ve", sch.now("ve"))])
                if c < 3:
                    emit_passA(e_i, 8 * c + 8)
                emit_passB(c)

        gate_ln2 = (dbg_dump(f"preln2_{st}", RES, NTT * 1024,
                             [("ve", ffn_done_m[NTT - 1])]) if st == 0 else 0)

        ln2_done = emit_ln([("ve", ffn_done_m[tt]) for tt in range(NTT)],
                           gate_act=gate_ln2)
        for tt in range(NTT):
            r0 = st * STW + tt * 128
            sch.op("sync",
                   (lambda sy, r0=r0, tt=tt:
                    sy.dma_start(out_h[r0:r0 + 128, :], RES[:, tt * 1024:(tt + 1) * 1024])),
                   waits=[("ve", ln2_done[tt])], incs=[("st", 16)])
            if st + 1 < NST:
                r1 = (st + 1) * STW + tt * 128
                sch.op("sync",
                       (lambda sy, r1=r1, tt=tt:
                        sy.dma_start(RES[:, tt * 1024:(tt + 1) * 1024], x_h[r1:r1 + 128, :])),
                       incs=[(f"x{tt}", 16)])

    for k in range(2, NK):
        emit_slab_load(k)

    dbg_dump("qc", QC, 36, [])
    dbg_dump("car", CAR, 16, [])
    dbg_dump("ms", MS, NTT * 8, [])
    sch.op("sync", lambda sy: None, waits=[("st", sch.now("st"))])

    with nc.Block() as block:
        def mk(engname):
            def run(eng):
                for fn, waits, incs in sch.ops[engname]:
                    for s, v in waits:
                        eng.wait_ge(sems[s], v)
                    instr = fn(eng)
                    if instr is not None:
                        for s, k in incs:
                            instr.then_inc(sems[s], k)
                    else:
                        for s, k in incs:
                            eng.sem_inc(sems[s], k)
            return run

        block.sync(mk("sync"))
        block.tensor(mk("tensor"))
        block.vector(mk("vector"))
        block.scalar(mk("scalar"))
        block.gpsimd(mk("gpsimd"))

    es.close()
    return nc


_CACHE = {}


def _prep_inputs(inputs):
    w1 = np.asarray(inputs["L0_ffn_w1"], dtype=np.float32)
    w2 = np.asarray(inputs["L0_ffn_w2"], dtype=np.float32)
    w1a = np.ascontiguousarray(
        w1[:, :, :4096].reshape(E, NDT, 128, NSLAB, 4, 128)
        .transpose(0, 3, 2, 4, 1, 5).reshape(E, NSLAB, 128, 4096))
    w1g = np.ascontiguousarray(
        w1[:, :, 4096:].reshape(E, NDT, 128, NSLAB, 4, 128)
        .transpose(0, 3, 2, 4, 1, 5).reshape(E, NSLAB, 128, 4096))
    w2l = np.ascontiguousarray(
        w2.reshape(E, NSLAB, 4, 128, 1024).transpose(0, 1, 3, 2, 4)
        .reshape(E, NSLAB, 128, 4096))
    rw = np.ascontiguousarray(
        np.asarray(inputs["L0_ffn_router_w"], dtype=np.float32)
        .reshape(NDT, 128, E).transpose(1, 0, 2).reshape(128, NDT * E))
    rbb = np.ascontiguousarray(
        np.broadcast_to(np.asarray(inputs["L0_ffn_router_b"], dtype=np.float32), (128, 8)))

    def chan(v):
        return (np.asarray(v, dtype=np.float32).reshape(D, 2)
                .reshape(NDT, 128, 2).transpose(1, 0, 2).reshape(128, 16))

    omg = (np.asarray(inputs["L0_attn_omega"], dtype=np.float32)
           .reshape(NDT, 128).T)
    prm = np.ascontiguousarray(np.concatenate(
        [chan(inputs["L0_attn_delta"]), chan(inputs["L0_attn_alpha"]),
         chan(inputs["L0_attn_beta"]), chan(inputs["L0_attn_gamma"]), omg], axis=1))
    idn = np.eye(128, dtype=np.float32)
    x = np.asarray(inputs["L0_attn_x"], dtype=np.float32)
    return x, w1a, w1g, w2l, rw, rbb, prm, idn


def kernel(**inputs):
    x, w1a, w1g, w2l, rw, rbb, prm, idn = _prep_inputs(inputs)
    if "nc" not in _CACHE:
        _CACHE["nc"] = build_program()
    nc = _CACHE["nc"]
    in_maps = []
    for b in range(8):
        in_maps.append({
            "x": np.ascontiguousarray(x[b]),
            "w1a": w1a, "w1g": w1g, "w2": w2l,
            "rw": rw, "rbb": rbb, "prm": prm, "idn": idn,
        })
    res = run_bass_kernel_spmd(nc, in_maps, core_ids=list(range(8)))
    out = np.stack([r["out"] for r in res.results], axis=0)
    return out.astype(np.float32)
